# version 03
# baseline (speedup 1.0000x reference)
import numpy as np
import concourse.bass as bass
import concourse.mybir as mybir
from concourse.bass_utils import run_bass_kernel_spmd

F32 = mybir.dt.float32
BF16 = mybir.dt.bfloat16
AF = mybir.ActivationFunctionType
ALU = mybir.AluOpType
AX = mybir.AxisListType
AP = bass.AP

NCORES = 8
BPC = 2
S = 2048
D = 1024
H = 16
CH = 512
NCH_SEQ = S // CH
NTILE = CH // 128
NCH = BPC * NCH_SEQ
EPS = 1e-5
LN8 = float(np.log(0.125))
LIMIT_CH = None
PHASES = "ABCDEF"
C_STEP = 99
C_HEADS = 99
NO_NORM = False
NO_V = False
NORM_TP0 = False
NO_GATHER = False
NO_NMM = False
NO_NCHAIN = False
NO_NRED = False
NO_TMAX = False
TMAX_COPY = False
TMAX_NIP = False
RED_SBUF = False
DBG = False


def _plan_program():
    nc = bass.Bass("TRN2", target_bir_lowering=False)

    xin = nc.dram_tensor("xin", [BPC * S, D], F32, kind="ExternalInput")
    wqk_h = nc.dram_tensor("wqk", [16, 8, 128, 128], F32, kind="ExternalInput")
    wv_h = nc.dram_tensor("wv", [2, 8, 128, 512], F32, kind="ExternalInput")
    wo_h = nc.dram_tensor("wo", [8, 128, 1024], F32, kind="ExternalInput")
    win_h = nc.dram_tensor("win", [8, 128, 64], F32, kind="ExternalInput")
    w1_h = nc.dram_tensor("w1", [64, 16], F32, kind="ExternalInput")
    b1_h = nc.dram_tensor("b1", [64, 16], F32, kind="ExternalInput")
    w2_h = nc.dram_tensor("w2", [64, 16], F32, kind="ExternalInput")
    bin_h = nc.dram_tensor("binv", [64, 1], F32, kind="ExternalInput")
    b2_h = nc.dram_tensor("b2v", [64, 1], F32, kind="ExternalInput")
    bout_h = nc.dram_tensor("boutv", [1, 1024], F32, kind="ExternalInput")
    wout_h = nc.dram_tensor("woutm", [64, 1024], F32, kind="ExternalInput")
    ident_h = nc.dram_tensor("ident", [128, 128], F32, kind="ExternalInput")
    dmask_h = nc.dram_tensor("dmask", [128, 896], F32, kind="ExternalInput")

    outx = nc.dram_tensor("outx", [BPC * S, D], F32, kind="ExternalOutput")
    if DBG:
        d_cbc = nc.dram_tensor("d_cbc", [128, 196], F32, kind="ExternalOutput")
        d_qt = nc.dram_tensor("d_qt", [128, 4096], F32, kind="ExternalOutput")
        d_kt = nc.dram_tensor("d_kt", [128, 4096], F32, kind="ExternalOutput")
        d_vst = nc.dram_tensor("d_vst", [128, 4160], F32, kind="ExternalOutput")
        d_au = nc.dram_tensor("d_au", [65, 2048], F32, kind="ExternalOutput")
        d_xlnT = nc.dram_tensor("d_xlnT", [128, 4096], F32, kind="ExternalOutput")
        d_p = nc.dram_tensor("d_p", [128, 4096], F32, kind="ExternalOutput")
        d_at = nc.dram_tensor("d_at", [128, 4096], F32, kind="ExternalOutput")
        d_x = nc.dram_tensor("d_x", [128, 4096], F32, kind="ExternalOutput")
        d_wq = nc.dram_tensor("d_wq", [128, 2048], F32, kind="ExternalOutput")
        d_wv = nc.dram_tensor("d_wv", [128, 4096], F32, kind="ExternalOutput")
        d_xln = nc.dram_tensor("d_xln", [128, 2048], F32, kind="ExternalOutput")
        d_mv = nc.dram_tensor("d_mv", [128, 4], F32, kind="ExternalOutput")
        d_rstd = nc.dram_tensor("d_rstd", [128, 2], F32, kind="ExternalOutput")
        d_tp = nc.dram_tensor("d_tp", [128, 1024], F32, kind="ExternalOutput")
        d_lnv = nc.dram_tensor("d_lnv", [128, 2], F32, kind="ExternalOutput")
        d_r2 = nc.dram_tensor("d_r2", [128, 1], F32, kind="ExternalOutput")
        d_eps = nc.dram_tensor("d_eps", [128, 2], F32, kind="ExternalOutput")

    rs_scr = nc.dram_tensor("rs_scr", [16, 512], F32, kind="Internal")
    rr_scr = nc.dram_tensor("rr_scr", [16, 512], F32, kind="Internal")
    scr98 = nc.dram_tensor("scr98", [98, 2], F32, kind="Internal")

    al = nc.alloc_sbuf_tensor
    woT = al("woT", [128, 8, 1024], BF16)
    wq_sb = al("wq_sb", [128, 2, 8, 128], BF16)
    wv_sb = al("wv_sb", [128, 2, 4, 512], BF16)
    win_bf = al("win_bf", [128, 8, 64], BF16)
    wout_bf = al("wout_bf", [64, 1024], BF16)
    w1_sb = al("w1_sb", [64, 16], F32)
    b1_sb = al("b1_sb", [64, 16], F32)
    b1p = al("b1p", [64, 16], F32)
    w2_sb = al("w2_sb", [64, 16], F32)
    bin_sb = al("bin_sb", [64, 1], F32)
    b2bf = al("b2bf", [64, 1], BF16)
    bout_sb = al("bout_sb", [1, 1024], F32)
    bout2 = al("bout2", [1, 1024], F32)
    b1t = al("b1t", [64, 16], F32)
    boute_bf = al("boute_bf", [1, 1024], BF16)
    ident_bf = al("ident_bf", [128, 128], BF16)
    dmask_bf = al("dmask_bf", [128, 896], BF16)
    ones1 = al("ones1", [1, 128], BF16)
    bones = al("bones", [128, 2], BF16)
    eps_t = al("eps_t", [128, 1], F32)
    ln8_t = al("ln8_t", [128, 1], F32)

    x_sb = al("x_sb", [128, 2, 4, 1024], F32)
    xln = al("xln", [128, 2, 1024], BF16)
    xlnT = al("xlnT", [128, 8, 512], BF16)
    QT = al("QT", [128, 8, 512], BF16)
    KT = al("KT", [128, 8, 2048], BF16)
    Vst = al("Vst", [128, 16, 16, 65], BF16)
    Pbuf = al("Pbuf", [128, 2, 4, 512], BF16)
    Pmsk = al("Pmsk", [128, 2, 512], BF16)
    attnU = al("attnU", [65, 4, 512], F32)
    attnT = al("attnT", [128, 8, 512], BF16)
    attnB = al("attnB", [64, 512], BF16)
    sq_sb = al("sq_sb", [128, 2, 512], BF16)
    rs_sb = al("rs_sb", [4, 512], F32)
    rr_sb = al("rr_sb", [4, 512], F32)
    rr2_sb = al("rr2_sb", [4, 512], F32)
    bcast = al("bcast", [64, 2, 512], F32)
    nmax = al("nmax", [128, 2, 4], F32)
    ntmp = al("ntmp", [128, 1], F32)
    nprod = al("nprod", [128, 2], F32)
    negc = al("negc", [128, 2], F32)
    negc2 = al("negc2", [128, 2], F32)
    lnp = al("lnp", [128, 2], F32)
    cbc = al("cbc", [128, 196], F32)
    bn1 = al("bn1", [128, 2, 12], F32)
    mv = al("mv", [128, 2, 2], F32)
    lnv = al("lnv", [128, 2, 1], F32)
    rstd = al("rstd", [128, 2, 1], F32)
    negmr = al("negmr", [128, 2, 1], F32)
    h_sb = al("h_sb", [64, 2, 512], BF16)
    g_sb = al("g_sb", [64, 2, 512], F32)
    gbf = al("gbf", [64, 512], BF16)

    ps = nc.alloc_psum_tensor("ps", [128, 3584], F32)
    ps_tp = nc.alloc_psum_tensor("ps_tp", [128, 1024], BF16)

    DMAQ = {"GP": "gpsimd", "WQ0": "gpsimd", "WQ1": "gpsimd",
            "WV0": "gpsimd", "WV1": "gpsimd",
            "SX0": "sync", "SX1": "sync", "SRS": "sync", "SRL": "sync",
            "SRW": "sync", "SBC0": "sync", "SBC1": "sync", "SSB": "sync",
            "SNC": "sync",
            "SCB": "sync", "SOUT": "sync"}
    QENG = {"T": "tensor", "V": "vector", "S": "scalar", "G": "gpsimd"}
    QENG.update(DMAQ)
    SEM = {k: nc.alloc_semaphore(k) for k in QENG}
    C = {k: 0 for k in SEM}
    plan = {k: [] for k in ("tensor", "vector", "scalar", "gpsimd", "sync")}

    def emit(inc, waits, f):
        w = [(k, v) for (k, v) in waits if v and v > 0]
        if inc in ("V", "S", "G") and C[inc] > 0:
            w.append((inc, C[inc]))

        def run(eng, w=w, inc=inc, f=f):
            for k, v in w:
                eng.wait_ge(SEM[k], v)
            ins = f(eng)
            ins.then_inc(SEM[inc], 16 if inc in DMAQ else 1)
        plan[QENG[inc]].append(run)
        C[inc] += 16 if inc in DMAQ else 1
        return C[inc]

    emit("GP", [], lambda g: g.dma_start(
        woT[:, :, :], AP(wo_h, 0, [[1024, 128], [131072, 8], [1, 1024]])))
    emit("GP", [], lambda g: g.dma_start(
        win_bf[:, :, :], AP(win_h, 0, [[64, 128], [8192, 8], [1, 64]])))
    emit("GP", [], lambda g: g.dma_start(wout_bf[:, :], wout_h[:, :]))
    emit("GP", [], lambda g: g.dma_start(w1_sb[:, :], w1_h[:, :]))
    emit("GP", [], lambda g: g.dma_start(b1_sb[:, :], b1_h[:, :]))
    emit("GP", [], lambda g: g.dma_start(w2_sb[:, :], w2_h[:, :]))
    emit("GP", [], lambda g: g.dma_start(bin_sb[:, :], bin_h[:, :]))
    emit("GP", [], lambda g: g.dma_start(b2bf[:, :], b2_h[:, :]))
    emit("GP", [], lambda g: g.dma_start(bout_sb[:, :], bout_h[:, :]))
    emit("GP", [], lambda g: g.dma_start(ident_bf[:, :], ident_h[:, :]))
    gd_consts = emit("GP", [], lambda g: g.dma_start(dmask_bf[:, :], dmask_h[:, :]))
    emit("G", [], lambda g: g.memset(ones1[:, :], 1.0))
    emit("G", [], lambda g: g.memset(
        AP(Vst, 64, [[16640, 128], [1040, 16], [65, 16], [1, 1]]), 1.0))
    emit("G", [], lambda g: g.memset(eps_t[:, :], EPS))
    emit("G", [], lambda g: g.memset(ln8_t[:, :], LN8))
    emit("G", [], lambda g: g.memset(Pbuf[:, :, :, :], 0.0))
    if DBG:
        emit("G", [], lambda g: g.memset(cbc[:, :], 0.0))
        emit("G", [], lambda g: g.memset(QT[:, :, :], 0.0))
        emit("G", [], lambda g: g.memset(KT[:, :, :], 0.0))
        emit("G", [], lambda g: g.memset(Vst[:, :, :, :], 0.0))
        emit("G", [], lambda g: g.memset(attnU[:, :, :], 0.0))
        emit("G", [], lambda g: g.memset(attnT[:, :, :], 0.0))
        emit("G", [], lambda g: g.memset(xlnT[:, :, :], 0.0))
        emit("G", [], lambda g: g.memset(wq_sb[:, :, :, :], 0.0))
        emit("G", [], lambda g: g.memset(wv_sb[:, :, :, :], 0.0))
    emit("G", [], lambda g: g.memset(bones[:, :], 0.0))
    emit("G", [], lambda g: g.memset(bones[0:64, 0:1], 1.0))
    g_pro = emit("G", [], lambda g: g.memset(bones[64:128, 1:2], 1.0))


    def x_load(ch):
        off = ch * CH * D
        slot = ch % 2

        def f(sy, off=off, slot=slot):
            return sy.dma_start(x_sb[:, slot, :, :],
                                AP(xin, off, [[1024, 128], [131072, 4], [1, 1024]]))
        return f
    xld = {0: emit("SX0", [], x_load(0)), 1: emit("SX1", [], x_load(1))}

    t_be = emit("T", [("GP", gd_consts)], lambda t: (
        t.matmul(ps[0:1, 0:512], b2bf[:, :], wout_bf[:, 0:512], start=True, stop=True),
        t.matmul(ps[0:1, 512:1024], b2bf[:, :], wout_bf[:, 512:1024],
                 start=True, stop=True))[-1])
    emit("V", [("T", t_be)],
         lambda v: v.tensor_add(bout2[:, :], bout_sb[:, :], ps[0:1, 0:1024]))
    emit("V", [], lambda v: v.tensor_copy(boute_bf[:, :], bout2[:, :]))
    emit("V", [("GP", gd_consts)],
         lambda v: v.tensor_scalar(b1t[:, :], w1_sb[:, :], bin_sb[:, 0:1], None, ALU.mult))
    v_pro = emit("V", [], lambda v: v.tensor_add(b1p[:, :], b1t[:, :], b1_sb[:, :]))
    v_psz = emit("V", [], lambda v: v.memset(ps[0:128, 2048:3072], 0.0))

    last_tp_evac = 0
    outproj_done_T = 0
    last_hS = 0
    rs_read_S = 0
    rr_stored_SY = 0
    last_F7_V = v_pro
    last_exp_S = 0
    xln_slot_T = [0, 0]
    mvS = [0, 0]
    b_slot_AV_T = [0, 0]
    bc_slot_G = [0, 0]
    hb_slot_V = [0, 0]
    stageB_SY = [0, 0]
    wq_slot_T = [0, 0]
    wv_slot_T = [0, 0]
    attnU_mult_G = [0, 0, 0, 0]
    pmsk_T = [0, 0]
    attn_bank_evac_V = [0, 0]
    evac_qk = {}
    v_evac = {}
    mult_cnt = {}

    for seq in range(BPC):
        g_nm = emit("G", [("V", C["V"])], lambda g: g.memset(nmax[:, :, :], 0.0))

        for gs in range(NCH_SEQ):
            ch = seq * NCH_SEQ + gs
            slot = ch % 2
            if LIMIT_CH is not None and ch >= LIMIT_CH:
                continue
            ph = PHASES if (LIMIT_CH is None or ch == LIMIT_CH - 1) else "ABCDEF"

            def ln_block(ti, first_wait):
                nonlocal last_tp_evac
                ms = ti % 2
                xv = x_sb[:, slot, ti, :]
                emit("V", first_wait + [("S", mvS[ms])],
                     lambda v, xv=xv, ms=ms: v.bn_stats(bn1[:, ms, 0:6], xv[:, 0:512]))
                emit("V", [],
                     lambda v, xv=xv, ms=ms: v.bn_stats(bn1[:, ms, 6:12], xv[:, 512:1024]))
                vc = emit("V", [], lambda v, ms=ms: v.bn_aggr(
                    mv[:, ms, :], bn1[:, ms, :].rearrange("p (g s) -> p g s", s=6)))
                emit("S", [("V", vc), ("G", g_pro)], lambda s, ms=ms: s.activation(
                    lnv[:, ms, :], mv[:, ms, 1:2], AF.Ln, bias=eps_t[:, 0:1], scale=1.0))
                mvS[ms] = emit("S", [], lambda s, ms=ms: s.activation(
                    rstd[:, ms, :], lnv[:, ms, :], AF.Exp, bias=0.0, scale=-0.5))
                emit("V", [("S", mvS[ms])], lambda v, ms=ms: v.tensor_scalar(
                    negmr[:, ms, :], mv[:, ms, 0:1], rstd[:, ms, 0:1], -1.0,
                    ALU.mult, ALU.mult))
                vx = emit("V", [("T", xln_slot_T[ms])],
                          lambda v, xv=xv, ms=ms: v.tensor_scalar(
                              xln[:, ms, :], xv, rstd[:, ms, 0:1], negmr[:, ms, 0:1],
                              ALU.mult, ALU.add))

                def tf(t, ms=ms):
                    ins = None
                    for c in range(8):
                        ins = t.transpose(ps_tp[:, c * 128:(c + 1) * 128],
                                          xln[:, ms, c * 128:(c + 1) * 128],
                                          ident_bf[:, :])
                    return ins
                xln_slot_T[ms] = emit("T", [("V", vx), ("V", last_tp_evac),
                                            ("GP", gd_consts)], tf)
                last_tp_evac = emit("V", [("T", xln_slot_T[ms])],
                                    lambda v, ti=ti: v.tensor_copy(
                                        AP(xlnT, ti * 128,
                                           [[4096, 128], [512, 8], [1, 128]]),
                                        ps_tp[:, 0:1024]))

            if "A" not in ph:
                continue
            for ti in range(NTILE):
                ln_block(ti, [("SX%d" % (ch % 2), xld[ch])] if ti == 0 else [])
            a_done_V = last_tp_evac
            if "B" not in ph:
                continue

            wq_gd = {}
            wq_gd[0] = emit("WQ0", [("T", wq_slot_T[0])],
                            lambda g: g.dma_start(
                                wq_sb[:, 0, :, :],
                                AP(wqk_h, 0, [[128, 128], [16384, 8], [1, 128]])))
            wv_gd = {}
            wv_gd[(0, 0)] = emit("WV0", [("T", wv_slot_T[0])],
                                 lambda g: g.dma_start(
                                     wv_sb[:, 0, :, :],
                                     AP(wv_h, 0, [[512, 128], [65536, 4], [1, 512]])))
            qn_done_V = [0, 0]
            nrm_T = [0, 0]
            for og in range(16):
                ob = og % 2
                if og < 15:
                    off = (og + 1) * 8 * 128 * 128
                    ob2 = (og + 1) % 2

                    def wf(g, off=off, ob2=ob2):
                        return g.dma_start(
                            wq_sb[:, ob2, :, :],
                            AP(wqk_h, off, [[128, 128], [16384, 8], [1, 128]]))
                    wq_gd[og + 1] = emit("WQ%d" % ob2, [("T", wq_slot_T[ob2])], wf)

                def qf(t, ob=ob):
                    ins = None
                    for c in range(8):
                        ins = t.matmul(ps[:, ob * 512:(ob + 1) * 512],
                                       wq_sb[:, ob, c, :], xlnT[:, c, :],
                                       start=(c == 0), stop=(c == 7))
                    return ins
                prev_bank = evac_qk.get(og - 2, evac_qk.get(og + 14, 0))
                waits = [("WQ%d" % (og % 2), wq_gd[og]), ("V", a_done_V),
                         ("V", prev_bank)]
                if og <= 1:
                    waits.append(("V", last_F7_V))
                wq_slot_T[ob] = emit("T", waits, qf)

                hp = og % 8
                if og < 8:
                    def ef(v, hp=hp, ob=ob):
                        return v.tensor_copy(QT[:, hp, :], ps[:, ob * 512:(ob + 1) * 512])
                    src = QT[:, hp, :]
                else:
                    def ef(v, hp=hp, ob=ob, gs=gs):
                        return v.tensor_copy(KT[:, hp, gs * 512:(gs + 1) * 512],
                                             ps[:, ob * 512:(ob + 1) * 512])
                    src = KT[:, hp, gs * 512:(gs + 1) * 512]
                evac_qk[og] = emit("V", [("T", wq_slot_T[ob])], ef)
                if NO_NORM:
                    continue
                gsq = emit("G", [("V", evac_qk[og])],
                           lambda g, src=src, ob=ob: g.tensor_mul(
                               sq_sb[:, ob, :], src, src))
                if NO_NMM:
                    continue
                half = hp // 4
                nb = 4 + half
                tpj = 32 * (hp % 4)

                def nf(t, nb=nb, tpj=tpj, ob=ob):
                    if NORM_TP0:
                        return t.matmul(ps[0:2, nb * 512:(nb + 1) * 512],
                                        bones[:, :], sq_sb[:, ob, :],
                                        start=True, stop=True)
                    return t.matmul(ps[tpj:tpj + 2, nb * 512:(nb + 1) * 512],
                                    bones[:, :], sq_sb[:, ob, :], start=True, stop=True,
                                    tile_position=(0, tpj))
                nw = [("G", gsq), ("V", v_psz)]
                if og >= 8:
                    nw.append(("V", qn_done_V[half]))
                nrm_T[half] = emit("T", nw, nf)
                if hp % 4 == 3 and not NO_NRED:
                    col = 0 if og < 8 else 1
                    bk = 4 + half

                    def rf(v, bk=bk, ob=ob):
                        if RED_SBUF:
                            return v.reduce_max(ntmp[:, :], sq_sb[:, ob, :], AX.X)
                        return v.reduce_max(ntmp[:, :], ps[0:128, bk * 512:(bk + 1) * 512],
                                            AX.X)
                    emit("V", [("T", nrm_T[half])], rf)
                    if NO_TMAX:
                        qn_done_V[half] = C["V"]
                        continue
                    cidx = col * 2 + half

                    def mxf(v, cidx=cidx, gs=gs):
                        return v.tensor_max(nmax[:, gs % 2, cidx:cidx + 1],
                                            nmax[:, (gs + 1) % 2, cidx:cidx + 1],
                                            ntmp[:, :])
                    qn_done_V[half] = emit("V", [("G", g_nm)], mxf)

            if NO_V:
                b_done_V = evac_qk[15]
                b_done_T = wq_slot_T[1]
                continue_v = True
            vmm_T = 0
            for vg in range(2 if not NO_V else 0):
                for hf in range(2):
                    if not (vg == 0 and hf == 0):
                        off = (vg * 8 + hf * 4) * 128 * 512

                        def vf(g, off=off, hf=hf):
                            return g.dma_start(
                                wv_sb[:, hf, :, :],
                                AP(wv_h, off, [[512, 128], [65536, 4], [1, 512]]))
                        wv_gd[(vg, hf)] = emit("WV%d" % hf, [("T", wv_slot_T[hf])], vf)
                    for ti in range(NTILE):
                        def vmf(t, hf=hf, ti=ti):
                            ins = None
                            for cc in range(4):
                                ins = t.matmul(
                                    ps[:, ti * 512:(ti + 1) * 512],
                                    xlnT[:, hf * 4 + cc, ti * 128:(ti + 1) * 128],
                                    wv_sb[:, hf, cc, :],
                                    start=(hf == 0 and cc == 0),
                                    stop=(hf == 1 and cc == 3))
                            return ins
                        w = [("WV%d" % hf, wv_gd[(vg, hf)])]
                        if hf == 0:
                            w.append(("V", v_evac.get((vg - 1, ti),
                                                      v_evac.get((1 - vg, ti), 0))))
                            if ti <= 1:
                                w += [("V", evac_qk[14]), ("V", evac_qk[15])]
                        wv_slot_T[hf] = emit("T", w, vmf)
                        if hf == 1:
                            vmm_T = wv_slot_T[1]

                            def vef(v, vg=vg, ti=ti, gs=gs):
                                j = gs * 4 + ti
                                return v.tensor_copy(
                                    AP(Vst, (vg * 8) * 1040 + j * 65,
                                       [[16640, 128], [1040, 8], [1, 64]]),
                                    ps[:, ti * 512:(ti + 1) * 512])
                            v_evac[(vg, ti)] = emit("V", [("T", vmm_T)], vef)
            if not NO_V:
                b_done_V = v_evac[(1, 3)]
                b_done_T = vmm_T

            if NO_NORM or NO_NMM or NO_NCHAIN or NO_NRED:
                bc_sy = C["SY"]
                vp = 0
            elif True:
                vp = emit("V", [("V", qn_done_V[1])],
                          lambda v, gs=gs: v.tensor_mul(
                              nprod[:, :], nmax[:, gs % 2, 0:2], nmax[:, gs % 2, 2:4]))
                emit("S", [("V", vp), ("G", g_pro)], lambda s: s.activation(
                    lnp[:, :], nprod[:, :], AF.Ln, bias=eps_t[:, 0:1], scale=1.0))
                sp = emit("S", [("G", g_pro)], lambda s: s.activation(
                    negc[:, :], lnp[:, :], AF.Exp, bias=ln8_t[0:128, 0:1], scale=0.5))
                vn = emit("V", [("S", sp)], lambda v: v.tensor_scalar(
                    negc2[:, :], negc[:, :], -1.0, None, ALU.mult))
                if not NO_GATHER:
                    emit("SNC", [("V", vn), ("SCB", C["SCB"])],
                         lambda sy: sy.dma_start(scr98[:, :], negc2[0:98, :]))

                if NO_GATHER:
                    bc_sy = C["SY"]
                else:
                    bc_sy = emit("SCB", [("SNC", C["SNC"]), ("S", last_exp_S)],
                                 lambda sy: sy.dma_start(
                        cbc[:, :], AP(scr98, 0, [[0, 128], [1, 196]])))

            if "C" not in ph:
                continue
            for h in range(H):
                if h >= C_HEADS:
                    break
                hp, ab = h // 2, h % 2
                pb = 64 * ab
                for b in range(gs + 1):
                    diag = (b == gs)
                    bs = b % 2

                    def sf(t, b=b, hp=hp, pb=pb):
                        ins = None
                        for jj in range(4):
                            j = 4 * b + jj
                            ins = t.matmul(ps[:, jj * 512:(jj + 1) * 512],
                                           KT[pb:pb + 64, hp, j * 128:(j + 1) * 128],
                                           QT[pb:pb + 64, hp, :],
                                           start=True, stop=True)
                        return ins
                    sc_T = emit("T", [("S", last_exp_S), ("V", b_done_V),
                                      ("G", g_pro)], sf)
                    if C_STEP < 2:
                        continue

                    ui = 64 * ((h // 2) % 4) + 2 * (h % 2) + (h // 8)

                    def xf(s, bs=bs, ui=ui):
                        return s.activation(
                            Pbuf[:, bs, :, :].rearrange("p a b -> p (a b)"),
                            ps[:, 0:2048], AF.Exp, bias=cbc[:, ui:ui + 1], scale=0.125)
                    last_exp_S = emit("S", [("T", sc_T), ("SCB", bc_sy),
                                            ("T", b_slot_AV_T[bs])], xf)
                    if C_STEP < 4:
                        continue
                    if diag:
                        for jj in range(4):
                            j = 4 * b + jj
                            pms = jj % 2

                            def mkf(g, bs=bs, jj=jj, pms=pms):
                                return g.tensor_mul(
                                    Pmsk[:, pms, :], Pbuf[:, bs, jj, :],
                                    dmask_bf[:, (3 - jj) * 128:(3 - jj) * 128 + 512])
                            mG = emit("G", [("S", last_exp_S),
                                            ("T", pmsk_T[pms])], mkf)

                            def aj(t, j=j, jj=jj, h=h, b=b, pms=pms):
                                ab4 = 4 + (h % 2)
                                return t.matmul(ps[0:65, ab4 * 512:(ab4 + 1) * 512],
                                                Vst[:, h, j, 0:65], Pmsk[:, pms, :],
                                                start=(b == 0 and jj == 0),
                                                stop=(jj == 3))
                            avw = [("G", mG)]
                            if b == 0 and jj == 0:
                                avw.append(("V", attn_bank_evac_V[h % 2]))
                            pmsk_T[pms] = emit("T", avw, aj)
                        b_slot_AV_T[bs] = pmsk_T[1]
                    else:
                        def af(t, b=b, bs=bs, h=h):
                            ins = None
                            ab4 = 4 + (h % 2)
                            for jj in range(4):
                                j = 4 * b + jj
                                ins = t.matmul(ps[0:65, ab4 * 512:(ab4 + 1) * 512],
                                               Vst[:, h, j, 0:65], Pbuf[:, bs, jj, :],
                                               start=(b == 0 and jj == 0),
                                               stop=False)
                            return ins
                        av_wait = [("S", last_exp_S)]
                        if b == 0:
                            av_wait.append(("V", attn_bank_evac_V[h % 2]))
                        b_slot_AV_T[bs] = emit("T", av_wait, af)
                if C_STEP < 5:
                    continue
                av_T = b_slot_AV_T[gs % 2]
                us = h % 4
                ab4 = 4 + (h % 2)

                def uf(v, us=us, ab4=ab4):
                    return v.tensor_copy(attnU[:, us, :],
                                         ps[0:65, ab4 * 512:(ab4 + 1) * 512])
                ue = emit("V", [("T", av_T), ("G", attnU_mult_G[us])], uf)
                attn_bank_evac_V[h % 2] = ue
                if C_STEP < 6:
                    continue

                def rsf(sy, us=us, h=h):
                    return sy.dma_start(rs_scr[h:h + 1, :], attnU[64:65, us, :])
                emit("SRS", [("V", ue), ("SRL", C["SRL"])], rsf)

                if C_STEP < 7:
                    continue
                if h % 4 == 3:
                    g4 = h // 4

                    def lf(sy, g4=g4):
                        return sy.dma_start(rs_sb[:, :], rs_scr[g4 * 4:(g4 + 1) * 4, :])
                    ld_sy = emit("SRL", [("S", rs_read_S), ("SRS", C["SRS"])], lf)
                    rs_read_S = emit("S", [("SRL", ld_sy)],
                                     lambda s: s.activation(rr_sb[:, :], rs_sb[:, :],
                                                            AF.Ln, bias=0.0, scale=1.0))
                    s2 = emit("S", [], lambda s: s.activation(
                        rr2_sb[:, :], rr_sb[:, :], AF.Exp, bias=0.0, scale=-1.0))

                    def stf(sy, g4=g4):
                        return sy.dma_start(rr_scr[g4 * 4:(g4 + 1) * 4, :], rr2_sb[:, :])
                    rr_stored_SY = emit("SRW", [("S", s2), ("SBC0", C["SBC0"]),
                                                ("SBC1", C["SBC1"])], stf)
                    if C_STEP < 8:
                        continue
                    for h2 in range(g4 * 4, g4 * 4 + 4):
                        bs2 = h2 % 2
                        hp2, ab2 = h2 // 2, h2 % 2
                        us2 = h2 % 4

                        def bf(sy, h2=h2, bs2=bs2):
                            return sy.dma_start(
                                bcast[:, bs2, :],
                                rr_scr[h2:h2 + 1, :].partition_broadcast(64))
                        bsy = emit("SBC%d" % bs2, [("G", bc_slot_G[bs2]),
                                                   ("SRW", C["SRW"])], bf)
                        if C_STEP < 9:
                            continue
                        if ab2 == 0:
                            dst = attnT[0:64, hp2, :]
                        else:
                            dst = attnB[:, :]

                        def mf(g, dst=dst, us2=us2, bs2=bs2):
                            return g.tensor_mul(dst, attnU[0:64, us2, :],
                                                bcast[:, bs2, :])
                        mw = [("SBC%d" % bs2, bsy), ("T", outproj_done_T)]
                        if ab2 == 1:
                            mw.append(("SSB", stageB_SY[0]))
                        mult_cnt[(ch, h2)] = emit("G", mw, mf)
                        bc_slot_G[bs2] = mult_cnt[(ch, h2)]
                        attnU_mult_G[us2] = mult_cnt[(ch, h2)]
                        if ab2 == 1:
                            def sbf(sy, hp2=hp2):
                                return sy.dma_start(attnT[64:128, hp2, :],
                                                    attnB[:, :])
                            stageB_SY[0] = emit(
                                "SSB", [("G", mult_cnt[(ch, h2)])], sbf)
            if (ch, 14) not in mult_cnt:
                continue
            attn_done_G = mult_cnt[(ch, 14)]
            attn_done_SY = max(stageB_SY)

            if "D" not in ph:
                continue
            d3_V = 0
            for ti in range(NTILE):
                def of(t, ti=ti):
                    ins = None
                    for hp in range(8):
                        for mg in range(2):
                            ins = t.matmul(ps[:, mg * 512:(mg + 1) * 512],
                                           attnT[:, hp, ti * 128:(ti + 1) * 128],
                                           woT[:, hp, mg * 512:(mg + 1) * 512],
                                           start=(hp == 0), stop=(hp == 7))
                    return ins
                ow = [("G", attn_done_G), ("SSB", attn_done_SY), ("S", last_exp_S),
                      ("V", d3_V)]
                outproj_done_T = emit("T", ow, of)
                op_T = outproj_done_T

                def df(v, ti=ti, slot=slot):
                    return v.tensor_add(x_sb[:, slot, ti, :], x_sb[:, slot, ti, :],
                                        ps[:, 0:1024])
                d3_V = emit("V", [("T", op_T)], df)

            if "E" not in ph:
                continue
            for ti in range(NTILE):
                ln_block(ti, [("V", d3_V)] if ti == 0 else [])
            e_done_V = last_tp_evac

            if "F" not in ph:
                continue
            def f1(t):
                ins = None
                for c in range(8):
                    ins = t.matmul(ps[0:64, 3072:3584], win_bf[:, c, :], xlnT[:, c, :],
                                   start=(c == 0), stop=(c == 7))
                return ins
            f1_T = emit("T", [("V", e_done_V), ("S", last_hS)], f1)
            for i in range(16):
                hs = i % 2

                def hf2(s, i=i, hs=hs):
                    return s.activation(h_sb[:, hs, :], ps[0:64, 3072:3584],
                                        AF.Relu, bias=b1p[:, i:i + 1],
                                        scale=w1_sb[:, i:i + 1])
                sH = emit("S", [("T", f1_T), ("V", hb_slot_V[hs])], hf2)
                last_hS = sH
                if i == 0:
                    mac_V = emit("V", [("S", sH)],
                                 lambda v: v.tensor_scalar(g_sb[:, 0, :], h_sb[:, 0, :],
                                                           w2_sb[:, 0:1], None, ALU.mult))
                else:
                    def mi(v, i=i, hs=hs):
                        return v.scalar_tensor_tensor(g_sb[:, i % 2, :], h_sb[:, hs, :],
                                                      w2_sb[:, i:i + 1],
                                                      g_sb[:, (i + 1) % 2, :],
                                                      ALU.mult, ALU.add)
                    mac_V = emit("V", [("S", sH)], mi)
                hb_slot_V[hs] = mac_V
            gb_V = emit("V", [], lambda v: v.tensor_copy(gbf[:, :], g_sb[:, 1, :]))
            f7_V = 0
            for ti in range(NTILE):
                def f6(t, ti=ti):
                    t.matmul(ps[:, 0:512], gbf[:, ti * 128:(ti + 1) * 128],
                             wout_bf[:, 0:512], start=True, stop=False)
                    t.matmul(ps[:, 512:1024], gbf[:, ti * 128:(ti + 1) * 128],
                             wout_bf[:, 512:1024], start=True, stop=False)
                    t.matmul(ps[:, 0:512], ones1[:, :], boute_bf[:, 0:512],
                             start=False, stop=True)
                    return t.matmul(ps[:, 512:1024], ones1[:, :], boute_bf[:, 512:1024],
                                    start=False, stop=True)
                f6_T = emit("T", [("V", gb_V), ("V", f7_V if ti > 0 else d3_V),
                                  ("G", g_pro)], f6)

                def f7(v, ti=ti, slot=slot):
                    return v.tensor_add(x_sb[:, slot, ti, :], x_sb[:, slot, ti, :],
                                        ps[:, 0:1024])
                f7_V = emit("V", [("T", f6_T)], f7)
            last_F7_V = f7_V

            off = ch * CH * D

            def f8(sy, off=off, slot=slot):
                return sy.dma_start(AP(outx, off, [[1024, 128], [131072, 4], [1, 1024]]),
                                    x_sb[:, slot, :, :])
            emit("SOUT", [("V", f7_V)], f8)
            if ch + 2 < NCH:
                xld[ch + 2] = emit("SX%d" % (ch % 2), [("SOUT", C["SOUT"])],
                                   x_load(ch + 2))

    if DBG:
        dw = [("V", C["V"]), ("S", C["S"]), ("G", C["G"]), ("T", C["T"])]
        emit("GP", dw, lambda g: g.dma_start(d_cbc[:, :], cbc[:, :]))
        emit("GP", [], lambda g: g.dma_start(d_qt[:, :], QT[:, :, :]))
        emit("GP", [], lambda g: g.dma_start(
            d_kt[:, :], AP(KT, 0, [[16384, 128], [2048, 8], [1, 512]])))
        emit("GP", [], lambda g: g.dma_start(
            d_vst[:, :], AP(Vst, 0, [[16640, 128], [1040, 16], [1, 260]])))
        emit("GP", [], lambda g: g.dma_start(
            d_au[:, :], AP(attnU, 0, [[2048, 65], [1, 2048]])))
        emit("GP", [], lambda g: g.dma_start(d_xlnT[:, :], xlnT[:, :, :]))
        emit("GP", [], lambda g: g.dma_start(
            d_p[:, :], AP(Pbuf, 0, [[4096, 128], [1, 4096]])))
        emit("GP", [], lambda g: g.dma_start(d_at[:, :], attnT[:, :, :]))
        emit("GP", [], lambda g: g.dma_start(
            d_x[:, :], AP(x_sb, 0, [[8192, 128], [1, 4096]])))
        emit("GP", [], lambda g: g.dma_start(d_wq[:, :], wq_sb[:, :, :, :]))
        emit("GP", [], lambda g: g.dma_start(d_wv[:, :], wv_sb[:, :, :, :]))
        emit("GP", [], lambda g: g.dma_start(d_xln[:, :], xln[:, :, :]))
        emit("GP", [], lambda g: g.dma_start(d_mv[:, :], mv[:, :, :]))
        emit("GP", [], lambda g: g.dma_start(d_rstd[:, :], rstd[:, :, :]))
        emit("GP", [], lambda g: g.dma_start(d_lnv[:, :], lnv[:, :, :]))
        rtest = al("rtest", [128, 1], F32)
        sr2 = emit("S", [("V", C["V"]), ("T", C["T"]), ("G", C["G"])],
                   lambda s: s.activation(rtest[:, :], lnv[:, 0, :], AF.Exp,
                                          bias=0.0, scale=-0.5))
        emit("GP", [("S", sr2)], lambda g: g.dma_start(d_r2[:, :], rtest[:, :]))
        emit("GP", [], lambda g: g.dma_start(d_eps[:, 0:1], eps_t[:, :], allow_slow_non_contiguous=True))
        emit("GP", [], lambda g: g.dma_start(d_eps[:, 1:2], ln8_t[:, :], allow_slow_non_contiguous=True))

    final_sy = C["SOUT"]

    def fin(eng):
        eng.wait_ge(SEM["SOUT"], final_sy)
        eng.nop()
    plan["sync"].append(fin)

    with nc.Block() as block:
        @block.tensor
        def _(t):
            for f in plan["tensor"]:
                f(t)

        @block.vector
        def _(v):
            for f in plan["vector"]:
                f(v)

        @block.scalar
        def _(s):
            for f in plan["scalar"]:
                f(s)

        @block.gpsimd
        def _(g):
            for f in plan["gpsimd"]:
                f(g)

        @block.sync
        def _(sy):
            for f in plan["sync"]:
                f(sy)

    return nc


def _host_layouts(inputs):
    w_qkv = np.asarray(inputs["L0_attn_w_qkv"], dtype=np.float32)
    w_o = np.asarray(inputs["L0_attn_w_o"], dtype=np.float32)
    Win = np.asarray(inputs["L0_ffn_Win"], dtype=np.float32)
    bin_ = np.asarray(inputs["L0_ffn_bin_"], dtype=np.float32)
    W1 = np.asarray(inputs["L0_ffn_W1"], dtype=np.float32)
    b1 = np.asarray(inputs["L0_ffn_b1"], dtype=np.float32)
    W2 = np.asarray(inputs["L0_ffn_W2"], dtype=np.float32)
    b2 = np.asarray(inputs["L0_ffn_b2"], dtype=np.float32)
    Wout = np.asarray(inputs["L0_ffn_Wout"], dtype=np.float32)
    bout = np.asarray(inputs["L0_ffn_bout"], dtype=np.float32)

    wq = w_qkv[0].reshape(8, 128, 1024)
    wk = w_qkv[1].reshape(8, 128, 1024)
    wqk = np.concatenate([wq, wk], axis=0)
    wqk = wqk.transpose(0, 2, 1).reshape(16, 8, 128, 128)
    wvm = w_qkv[2].reshape(1024, 1024).T
    wv = wvm.reshape(8, 128, 2, 512).transpose(2, 0, 1, 3)
    wo = w_o.T.reshape(8, 128, 1024)
    win = Win.reshape(8, 128, 64)

    ident = np.eye(128, dtype=np.float32)
    p = np.arange(128)[:, None]
    u = np.arange(896)[None, :]
    dmask = (u - p >= 384).astype(np.float32)

    return {
        "wqk": np.ascontiguousarray(wqk),
        "wv": np.ascontiguousarray(wv),
        "wo": np.ascontiguousarray(wo),
        "win": np.ascontiguousarray(win),
        "w1": np.ascontiguousarray(W1),
        "b1": np.ascontiguousarray(b1),
        "w2": np.ascontiguousarray(W2),
        "binv": np.ascontiguousarray(bin_.reshape(64, 1)),
        "b2v": np.ascontiguousarray(b2.reshape(64, 1)),
        "boutv": np.ascontiguousarray(bout.reshape(1, 1024)),
        "woutm": np.ascontiguousarray(Wout),
        "ident": ident,
        "dmask": np.ascontiguousarray(dmask),
    }


def kernel(**inputs):
    x = np.asarray(inputs["L0_attn_x"], dtype=np.float32)
    shared = _host_layouts(inputs)
    nc = _plan_program()
    in_maps = []
    for c in range(NCORES):
        m = dict(shared)
        m["xin"] = np.ascontiguousarray(x[c * BPC:(c + 1) * BPC].reshape(BPC * S, D))
        in_maps.append(m)
    res = run_bass_kernel_spmd(nc, in_maps, core_ids=list(range(NCORES)))
    out = np.concatenate(
        [np.asarray(r["outx"]).reshape(BPC, S, D) for r in res.results], axis=0)
    return out


# version 04
# speedup vs baseline: 1.1418x; 1.1418x over previous
import numpy as np
import concourse.bass as bass
import concourse.mybir as mybir
from concourse.bass_utils import run_bass_kernel_spmd

F32 = mybir.dt.float32
BF16 = mybir.dt.bfloat16
AF = mybir.ActivationFunctionType
ALU = mybir.AluOpType
AX = mybir.AxisListType
AP = bass.AP

NCORES = 8
BPC = 2
S = 2048
D = 1024
H = 16
CH = 512
NCH_SEQ = S // CH
NTILE = CH // 128
NCH = BPC * NCH_SEQ
EPS = 1e-5
LN8 = float(np.log(0.125))
LIMIT_CH = None
PHASES = "ABCDEF"
C_STEP = 99
C_HEADS = 99
NO_NORM = False
NO_V = False
NORM_TP0 = False
NO_GATHER = False
NO_NMM = False
NO_NCHAIN = False
NO_NRED = False
NO_TMAX = False
TMAX_COPY = False
TMAX_NIP = False
RED_SBUF = False
DBG = False


def _plan_program():
    nc = bass.Bass("TRN2", target_bir_lowering=False)

    xin = nc.dram_tensor("xin", [BPC * S, D], F32, kind="ExternalInput")
    wqk_h = nc.dram_tensor("wqk", [16, 8, 128, 128], F32, kind="ExternalInput")
    wv_h = nc.dram_tensor("wv", [2, 8, 128, 512], F32, kind="ExternalInput")
    wo_h = nc.dram_tensor("wo", [8, 128, 1024], F32, kind="ExternalInput")
    win_h = nc.dram_tensor("win", [8, 128, 64], F32, kind="ExternalInput")
    w1_h = nc.dram_tensor("w1", [64, 16], F32, kind="ExternalInput")
    b1_h = nc.dram_tensor("b1", [64, 16], F32, kind="ExternalInput")
    w2_h = nc.dram_tensor("w2", [64, 16], F32, kind="ExternalInput")
    bin_h = nc.dram_tensor("binv", [64, 1], F32, kind="ExternalInput")
    b2_h = nc.dram_tensor("b2v", [64, 1], F32, kind="ExternalInput")
    bout_h = nc.dram_tensor("boutv", [1, 1024], F32, kind="ExternalInput")
    wout_h = nc.dram_tensor("woutm", [64, 1024], F32, kind="ExternalInput")
    ident_h = nc.dram_tensor("ident", [128, 128], F32, kind="ExternalInput")
    dmask_h = nc.dram_tensor("dmask", [128, 512], F32, kind="ExternalInput")

    outx = nc.dram_tensor("outx", [BPC * S, D], F32, kind="ExternalOutput")
    if DBG:
        d_cbc = nc.dram_tensor("d_cbc", [128, 196], F32, kind="ExternalOutput")
        d_qt = nc.dram_tensor("d_qt", [128, 4096], F32, kind="ExternalOutput")
        d_kt = nc.dram_tensor("d_kt", [128, 4096], F32, kind="ExternalOutput")
        d_vst = nc.dram_tensor("d_vst", [128, 4160], F32, kind="ExternalOutput")
        d_au = nc.dram_tensor("d_au", [65, 2048], F32, kind="ExternalOutput")
        d_xlnT = nc.dram_tensor("d_xlnT", [128, 4096], F32, kind="ExternalOutput")
        d_p = nc.dram_tensor("d_p", [128, 4096], F32, kind="ExternalOutput")
        d_at = nc.dram_tensor("d_at", [128, 4096], F32, kind="ExternalOutput")
        d_x = nc.dram_tensor("d_x", [128, 4096], F32, kind="ExternalOutput")
        d_wq = nc.dram_tensor("d_wq", [128, 2048], F32, kind="ExternalOutput")
        d_wv = nc.dram_tensor("d_wv", [128, 4096], F32, kind="ExternalOutput")
        d_xln = nc.dram_tensor("d_xln", [128, 2048], F32, kind="ExternalOutput")
        d_mv = nc.dram_tensor("d_mv", [128, 4], F32, kind="ExternalOutput")
        d_rstd = nc.dram_tensor("d_rstd", [128, 2], F32, kind="ExternalOutput")
        d_tp = nc.dram_tensor("d_tp", [128, 1024], F32, kind="ExternalOutput")
        d_lnv = nc.dram_tensor("d_lnv", [128, 2], F32, kind="ExternalOutput")
        d_r2 = nc.dram_tensor("d_r2", [128, 1], F32, kind="ExternalOutput")
        d_eps = nc.dram_tensor("d_eps", [128, 2], F32, kind="ExternalOutput")

    rs_scr = nc.dram_tensor("rs_scr", [16, 512], F32, kind="Internal")
    rr_scr = nc.dram_tensor("rr_scr", [16, 512], F32, kind="Internal")
    scr98 = nc.dram_tensor("scr98", [98, 2], F32, kind="Internal")

    al = nc.alloc_sbuf_tensor
    woT = al("woT", [128, 8, 1024], BF16)
    wq_sb = al("wq_sb", [128, 2, 8, 128], BF16)
    wv_sb = al("wv_sb", [128, 2, 4, 512], BF16)
    win_bf = al("win_bf", [128, 8, 64], BF16)
    wout_bf = al("wout_bf", [64, 1024], BF16)
    w1_sb = al("w1_sb", [64, 16], F32)
    b1_sb = al("b1_sb", [64, 16], F32)
    b1p = al("b1p", [64, 16], F32)
    w2_sb = al("w2_sb", [64, 16], F32)
    bin_sb = al("bin_sb", [64, 1], F32)
    b2bf = al("b2bf", [64, 1], BF16)
    bout_sb = al("bout_sb", [1, 1024], F32)
    bout2 = al("bout2", [1, 1024], F32)
    b1t = al("b1t", [64, 16], F32)
    boute_bf = al("boute_bf", [1, 1024], BF16)
    ident_bf = al("ident_bf", [128, 128], BF16)
    dmask_bf = al("dmask_bf", [128, 512], BF16)
    ones1 = al("ones1", [1, 128], BF16)
    bones = al("bones", [128, 2], BF16)
    eps_t = al("eps_t", [128, 1], F32)
    ln8_t = al("ln8_t", [128, 1], F32)

    x_sb = al("x_sb", [128, 2, 4, 1024], F32)
    xln = al("xln", [128, 2, 1024], BF16)
    xlnT = al("xlnT", [128, 8, 512], BF16)
    QT = al("QT", [128, 8, 512], BF16)
    KT = al("KT", [128, 8, 2048], BF16)
    Vst = al("Vst", [128, 16, 16, 65], BF16)
    Pbuf = al("Pbuf", [128, 2, 4, 512], BF16)
    attnU = al("attnU", [65, 4, 512], F32)
    attnT = al("attnT", [128, 8, 512], BF16)
    attnB = al("attnB", [64, 512], BF16)
    Ptri = al("Ptri", [128, 4, 128], BF16)
    sq_sb = al("sq_sb", [128, 2, 512], BF16)
    rs_sb = al("rs_sb", [4, 512], F32)
    rr_sb = al("rr_sb", [4, 512], F32)
    rr2_sb = al("rr2_sb", [4, 512], F32)
    bcast = al("bcast", [64, 2, 512], F32)
    nmax = al("nmax", [128, 2, 4], F32)
    ntmp = al("ntmp", [128, 1], F32)
    nprod = al("nprod", [128, 2], F32)
    negc = al("negc", [128, 2], F32)
    negc2 = al("negc2", [128, 2], F32)
    lnp = al("lnp", [128, 2], F32)
    cbc = al("cbc", [128, 196], F32)
    bn1 = al("bn1", [128, 2, 12], F32)
    mv = al("mv", [128, 2, 2], F32)
    lnv = al("lnv", [128, 2, 1], F32)
    rstd = al("rstd", [128, 2, 1], F32)
    negmr = al("negmr", [128, 2, 1], F32)
    h_sb = al("h_sb", [64, 2, 512], BF16)
    g_sb = al("g_sb", [64, 2, 512], F32)
    gbf = al("gbf", [64, 512], BF16)

    ps = nc.alloc_psum_tensor("ps", [128, 3584], F32)
    ps_tp = nc.alloc_psum_tensor("ps_tp", [128, 1024], BF16)

    DMAQ = {"GP": "gpsimd", "WQ0": "gpsimd", "WQ1": "gpsimd",
            "WV0": "gpsimd", "WV1": "gpsimd",
            "SX0": "sync", "SX1": "sync", "SRS": "sync", "SRL": "sync",
            "SRW": "sync", "SBC0": "sync", "SBC1": "sync", "SSB": "sync",
            "SNC": "sync",
            "SCB": "sync", "SOUT": "sync"}
    QENG = {"T": "tensor", "V": "vector", "S": "scalar", "G": "gpsimd"}
    QENG.update(DMAQ)
    SEM = {k: nc.alloc_semaphore(k) for k in QENG}
    C = {k: 0 for k in SEM}
    plan = {k: [] for k in ("tensor", "vector", "scalar", "gpsimd", "sync")}

    def emit(inc, waits, f):
        w = [(k, v) for (k, v) in waits if v and v > 0]
        if inc in ("V", "S", "G") and C[inc] > 0:
            w.append((inc, C[inc]))

        def run(eng, w=w, inc=inc, f=f):
            for k, v in w:
                eng.wait_ge(SEM[k], v)
            ins = f(eng)
            ins.then_inc(SEM[inc], 16 if inc in DMAQ else 1)
        plan[QENG[inc]].append(run)
        C[inc] += 16 if inc in DMAQ else 1
        return C[inc]

    emit("GP", [], lambda g: g.dma_start(
        woT[:, :, :], AP(wo_h, 0, [[1024, 128], [131072, 8], [1, 1024]])))
    emit("GP", [], lambda g: g.dma_start(
        win_bf[:, :, :], AP(win_h, 0, [[64, 128], [8192, 8], [1, 64]])))
    emit("GP", [], lambda g: g.dma_start(wout_bf[:, :], wout_h[:, :]))
    emit("GP", [], lambda g: g.dma_start(w1_sb[:, :], w1_h[:, :]))
    emit("GP", [], lambda g: g.dma_start(b1_sb[:, :], b1_h[:, :]))
    emit("GP", [], lambda g: g.dma_start(w2_sb[:, :], w2_h[:, :]))
    emit("GP", [], lambda g: g.dma_start(bin_sb[:, :], bin_h[:, :]))
    emit("GP", [], lambda g: g.dma_start(b2bf[:, :], b2_h[:, :]))
    emit("GP", [], lambda g: g.dma_start(bout_sb[:, :], bout_h[:, :]))
    emit("GP", [], lambda g: g.dma_start(ident_bf[:, :], ident_h[:, :]))
    gd_consts = emit("GP", [], lambda g: g.dma_start(dmask_bf[:, :], dmask_h[:, :]))
    emit("G", [], lambda g: g.memset(ones1[:, :], 1.0))
    emit("G", [], lambda g: g.memset(
        AP(Vst, 64, [[16640, 128], [1040, 16], [65, 16], [1, 1]]), 1.0))
    emit("G", [], lambda g: g.memset(eps_t[:, :], EPS))
    emit("G", [], lambda g: g.memset(ln8_t[:, :], LN8))
    emit("G", [], lambda g: g.memset(Pbuf[:, :, :, :], 0.0))
    if DBG:
        emit("G", [], lambda g: g.memset(cbc[:, :], 0.0))
        emit("G", [], lambda g: g.memset(QT[:, :, :], 0.0))
        emit("G", [], lambda g: g.memset(KT[:, :, :], 0.0))
        emit("G", [], lambda g: g.memset(Vst[:, :, :, :], 0.0))
        emit("G", [], lambda g: g.memset(attnU[:, :, :], 0.0))
        emit("G", [], lambda g: g.memset(attnT[:, :, :], 0.0))
        emit("G", [], lambda g: g.memset(xlnT[:, :, :], 0.0))
        emit("G", [], lambda g: g.memset(wq_sb[:, :, :, :], 0.0))
        emit("G", [], lambda g: g.memset(wv_sb[:, :, :, :], 0.0))
    emit("G", [], lambda g: g.memset(bones[:, :], 0.0))
    emit("G", [], lambda g: g.memset(bones[0:64, 0:1], 1.0))
    g_pro = emit("G", [], lambda g: g.memset(bones[64:128, 1:2], 1.0))


    def x_load(ch):
        off = ch * CH * D
        slot = ch % 2

        def f(sy, off=off, slot=slot):
            return sy.dma_start(x_sb[:, slot, :, :],
                                AP(xin, off, [[1024, 128], [131072, 4], [1, 1024]]))
        return f
    xld = {0: emit("SX0", [], x_load(0)), 1: emit("SX1", [], x_load(1))}

    t_be = emit("T", [("GP", gd_consts)], lambda t: (
        t.matmul(ps[0:1, 0:512], b2bf[:, :], wout_bf[:, 0:512], start=True, stop=True),
        t.matmul(ps[0:1, 512:1024], b2bf[:, :], wout_bf[:, 512:1024],
                 start=True, stop=True))[-1])
    emit("V", [("T", t_be)],
         lambda v: v.tensor_add(bout2[:, :], bout_sb[:, :], ps[0:1, 0:1024]))
    emit("V", [], lambda v: v.tensor_copy(boute_bf[:, :], bout2[:, :]))
    emit("V", [("GP", gd_consts)],
         lambda v: v.tensor_scalar(b1t[:, :], w1_sb[:, :], bin_sb[:, 0:1], None, ALU.mult))
    v_pro = emit("V", [], lambda v: v.tensor_add(b1p[:, :], b1t[:, :], b1_sb[:, :]))
    v_psz = emit("V", [], lambda v: v.memset(ps[0:128, 2048:3072], 0.0))

    last_tp_evac = 0
    outproj_done_T = 0
    last_hS = 0
    rs_read_S = 0
    rr_stored_SY = 0
    last_F7_V = v_pro
    last_exp_S = 0
    xln_slot_T = [0, 0]
    mvS = [0, 0]
    b_slot_AV_T = [0, 0]
    bc_slot_G = [0, 0]
    hb_slot_V = [0, 0]
    stageB_SY = [0, 0]
    wq_slot_T = [0, 0]
    wv_slot_T = [0, 0]
    attnU_mult_G = [0, 0, 0, 0]
    pmsk_V = 0
    ptri_T = 0
    attn_bank_evac_V = [0, 0]
    evac_qk = {}
    v_evac = {}
    mult_cnt = {}

    for seq in range(BPC):
        g_nm = emit("G", [("V", C["V"])], lambda g: g.memset(nmax[:, :, :], 0.0))

        for gs in range(NCH_SEQ):
            ch = seq * NCH_SEQ + gs
            slot = ch % 2
            if LIMIT_CH is not None and ch >= LIMIT_CH:
                continue
            ph = PHASES if (LIMIT_CH is None or ch == LIMIT_CH - 1) else "ABCDEF"

            def ln_block(ti, first_wait):
                nonlocal last_tp_evac
                ms = ti % 2
                xv = x_sb[:, slot, ti, :]
                emit("V", first_wait + [("S", mvS[ms])],
                     lambda v, xv=xv, ms=ms: v.bn_stats(bn1[:, ms, 0:6], xv[:, 0:512]))
                emit("V", [],
                     lambda v, xv=xv, ms=ms: v.bn_stats(bn1[:, ms, 6:12], xv[:, 512:1024]))
                vc = emit("V", [], lambda v, ms=ms: v.bn_aggr(
                    mv[:, ms, :], bn1[:, ms, :].rearrange("p (g s) -> p g s", s=6)))
                emit("S", [("V", vc), ("G", g_pro)], lambda s, ms=ms: s.activation(
                    lnv[:, ms, :], mv[:, ms, 1:2], AF.Ln, bias=eps_t[:, 0:1], scale=1.0))
                mvS[ms] = emit("S", [], lambda s, ms=ms: s.activation(
                    rstd[:, ms, :], lnv[:, ms, :], AF.Exp, bias=0.0, scale=-0.5))
                emit("V", [("S", mvS[ms])], lambda v, ms=ms: v.tensor_scalar(
                    negmr[:, ms, :], mv[:, ms, 0:1], rstd[:, ms, 0:1], -1.0,
                    ALU.mult, ALU.mult))
                vx = emit("V", [("T", xln_slot_T[ms])],
                          lambda v, xv=xv, ms=ms: v.tensor_scalar(
                              xln[:, ms, :], xv, rstd[:, ms, 0:1], negmr[:, ms, 0:1],
                              ALU.mult, ALU.add))

                def tf(t, ms=ms):
                    ins = None
                    for c in range(8):
                        ins = t.transpose(ps_tp[:, c * 128:(c + 1) * 128],
                                          xln[:, ms, c * 128:(c + 1) * 128],
                                          ident_bf[:, :])
                    return ins
                xln_slot_T[ms] = emit("T", [("V", vx), ("V", last_tp_evac),
                                            ("GP", gd_consts)], tf)
                last_tp_evac = emit("V", [("T", xln_slot_T[ms])],
                                    lambda v, ti=ti: v.tensor_copy(
                                        AP(xlnT, ti * 128,
                                           [[4096, 128], [512, 8], [1, 128]]),
                                        ps_tp[:, 0:1024]))

            if "A" not in ph:
                continue
            for ti in range(NTILE):
                ln_block(ti, [("SX%d" % (ch % 2), xld[ch])] if ti == 0 else [])
            a_done_V = last_tp_evac
            if "B" not in ph:
                continue

            wq_gd = {}
            wq_gd[0] = emit("WQ0", [("T", wq_slot_T[0])],
                            lambda g: g.dma_start(
                                wq_sb[:, 0, :, :],
                                AP(wqk_h, 0, [[128, 128], [16384, 8], [1, 128]])))
            wv_gd = {}
            wv_gd[(0, 0)] = emit("WV0", [("T", wv_slot_T[0])],
                                 lambda g: g.dma_start(
                                     wv_sb[:, 0, :, :],
                                     AP(wv_h, 0, [[512, 128], [65536, 4], [1, 512]])))
            qn_done_V = [0, 0]
            nrm_T = [0, 0]
            for og in range(16):
                ob = og % 2
                if og < 15:
                    off = (og + 1) * 8 * 128 * 128
                    ob2 = (og + 1) % 2

                    def wf(g, off=off, ob2=ob2):
                        return g.dma_start(
                            wq_sb[:, ob2, :, :],
                            AP(wqk_h, off, [[128, 128], [16384, 8], [1, 128]]))
                    wq_gd[og + 1] = emit("WQ%d" % ob2, [("T", wq_slot_T[ob2])], wf)

                def qf(t, ob=ob):
                    ins = None
                    for c in range(8):
                        ins = t.matmul(ps[:, ob * 512:(ob + 1) * 512],
                                       wq_sb[:, ob, c, :], xlnT[:, c, :],
                                       start=(c == 0), stop=(c == 7))
                    return ins
                prev_bank = evac_qk.get(og - 2, evac_qk.get(og + 14, 0))
                waits = [("WQ%d" % (og % 2), wq_gd[og]), ("V", a_done_V),
                         ("V", prev_bank)]
                if og <= 1:
                    waits.append(("V", last_F7_V))
                wq_slot_T[ob] = emit("T", waits, qf)

                hp = og % 8
                if og < 8:
                    def ef(v, hp=hp, ob=ob):
                        return v.tensor_copy(QT[:, hp, :], ps[:, ob * 512:(ob + 1) * 512])
                    src = QT[:, hp, :]
                else:
                    def ef(v, hp=hp, ob=ob, gs=gs):
                        return v.tensor_copy(KT[:, hp, gs * 512:(gs + 1) * 512],
                                             ps[:, ob * 512:(ob + 1) * 512])
                    src = KT[:, hp, gs * 512:(gs + 1) * 512]
                evac_qk[og] = emit("V", [("T", wq_slot_T[ob])], ef)
                if NO_NORM:
                    continue
                gsq = emit("G", [("V", evac_qk[og])],
                           lambda g, src=src, ob=ob: g.tensor_mul(
                               sq_sb[:, ob, :], src, src))
                if NO_NMM:
                    continue
                half = hp // 4
                nb = 4 + half
                tpj = 32 * (hp % 4)

                def nf(t, nb=nb, tpj=tpj, ob=ob):
                    if NORM_TP0:
                        return t.matmul(ps[0:2, nb * 512:(nb + 1) * 512],
                                        bones[:, :], sq_sb[:, ob, :],
                                        start=True, stop=True)
                    return t.matmul(ps[tpj:tpj + 2, nb * 512:(nb + 1) * 512],
                                    bones[:, :], sq_sb[:, ob, :], start=True, stop=True,
                                    tile_position=(0, tpj))
                nw = [("G", gsq), ("V", v_psz)]
                if og >= 8:
                    nw.append(("V", qn_done_V[half]))
                nrm_T[half] = emit("T", nw, nf)
                if hp % 4 == 3 and not NO_NRED:
                    col = 0 if og < 8 else 1
                    bk = 4 + half

                    def rf(v, bk=bk, ob=ob):
                        if RED_SBUF:
                            return v.reduce_max(ntmp[:, :], sq_sb[:, ob, :], AX.X)
                        return v.reduce_max(ntmp[:, :], ps[0:128, bk * 512:(bk + 1) * 512],
                                            AX.X)
                    emit("V", [("T", nrm_T[half])], rf)
                    if NO_TMAX:
                        qn_done_V[half] = C["V"]
                        continue
                    cidx = col * 2 + half

                    def mxf(v, cidx=cidx, gs=gs):
                        return v.tensor_max(nmax[:, gs % 2, cidx:cidx + 1],
                                            nmax[:, (gs + 1) % 2, cidx:cidx + 1],
                                            ntmp[:, :])
                    qn_done_V[half] = emit("V", [("G", g_nm)], mxf)

            if NO_V:
                b_done_V = evac_qk[15]
                b_done_T = wq_slot_T[1]
                continue_v = True
            vmm_T = 0
            for vg in range(2 if not NO_V else 0):
                for hf in range(2):
                    if not (vg == 0 and hf == 0):
                        off = (vg * 8 + hf * 4) * 128 * 512

                        def vf(g, off=off, hf=hf):
                            return g.dma_start(
                                wv_sb[:, hf, :, :],
                                AP(wv_h, off, [[512, 128], [65536, 4], [1, 512]]))
                        wv_gd[(vg, hf)] = emit("WV%d" % hf, [("T", wv_slot_T[hf])], vf)
                    for ti in range(NTILE):
                        def vmf(t, hf=hf, ti=ti):
                            ins = None
                            for cc in range(4):
                                ins = t.matmul(
                                    ps[:, ti * 512:(ti + 1) * 512],
                                    xlnT[:, hf * 4 + cc, ti * 128:(ti + 1) * 128],
                                    wv_sb[:, hf, cc, :],
                                    start=(hf == 0 and cc == 0),
                                    stop=(hf == 1 and cc == 3))
                            return ins
                        w = [("WV%d" % hf, wv_gd[(vg, hf)])]
                        if hf == 0:
                            w.append(("V", v_evac.get((vg - 1, ti),
                                                      v_evac.get((1 - vg, ti), 0))))
                            if ti <= 1:
                                w += [("V", evac_qk[14]), ("V", evac_qk[15])]
                        wv_slot_T[hf] = emit("T", w, vmf)
                        if hf == 1:
                            vmm_T = wv_slot_T[1]

                            def vef(v, vg=vg, ti=ti, gs=gs):
                                j = gs * 4 + ti
                                return v.tensor_copy(
                                    AP(Vst, (vg * 8) * 1040 + j * 65,
                                       [[16640, 128], [1040, 8], [1, 64]]),
                                    ps[:, ti * 512:(ti + 1) * 512])
                            v_evac[(vg, ti)] = emit("V", [("T", vmm_T)], vef)
            if not NO_V:
                b_done_V = v_evac[(1, 3)]
                b_done_T = vmm_T

            if NO_NORM or NO_NMM or NO_NCHAIN or NO_NRED:
                bc_sy = C["SY"]
                vp = 0
            elif True:
                vp = emit("V", [("V", qn_done_V[1])],
                          lambda v, gs=gs: v.tensor_mul(
                              nprod[:, :], nmax[:, gs % 2, 0:2], nmax[:, gs % 2, 2:4]))
                emit("S", [("V", vp), ("G", g_pro)], lambda s: s.activation(
                    lnp[:, :], nprod[:, :], AF.Ln, bias=eps_t[:, 0:1], scale=1.0))
                sp = emit("S", [("G", g_pro)], lambda s: s.activation(
                    negc[:, :], lnp[:, :], AF.Exp, bias=ln8_t[0:128, 0:1], scale=0.5))
                vn = emit("V", [("S", sp)], lambda v: v.tensor_scalar(
                    negc2[:, :], negc[:, :], -1.0, None, ALU.mult))
                if not NO_GATHER:
                    emit("SNC", [("V", vn), ("SCB", C["SCB"])],
                         lambda sy: sy.dma_start(scr98[:, :], negc2[0:98, :]))

                if NO_GATHER:
                    bc_sy = C["SY"]
                else:
                    bc_sy = emit("SCB", [("SNC", C["SNC"]), ("S", last_exp_S)],
                                 lambda sy: sy.dma_start(
                        cbc[:, :], AP(scr98, 0, [[0, 128], [1, 196]])))

            if "C" not in ph:
                continue
            for h in range(H):
                if h >= C_HEADS:
                    break
                hp, ab = h // 2, h % 2
                pb = 64 * ab
                for b in range(gs + 1):
                    diag = (b == gs)
                    bs = b % 2

                    def sf(t, b=b, hp=hp, pb=pb, diag=diag):
                        ins = None
                        for jj in range(4):
                            j = 4 * b + jj
                            q0 = jj * 128 if diag else 0
                            ins = t.matmul(ps[:, jj * 512 + q0:(jj + 1) * 512],
                                           KT[pb:pb + 64, hp, j * 128:(j + 1) * 128],
                                           QT[pb:pb + 64, hp, q0:512],
                                           start=True, stop=True)
                        return ins
                    sc_T = emit("T", [("S", last_exp_S), ("V", b_done_V),
                                      ("G", g_pro)], sf)
                    if C_STEP < 2:
                        continue

                    ui = 64 * ((h // 2) % 4) + 2 * (h % 2) + (h // 8)

                    def xf(s, bs=bs, ui=ui):
                        return s.activation(
                            Pbuf[:, bs, :, :].rearrange("p a b -> p (a b)"),
                            ps[:, 0:2048], AF.Exp, bias=cbc[:, ui:ui + 1], scale=0.125)
                    last_exp_S = emit("S", [("T", sc_T), ("SCB", bc_sy),
                                            ("T", b_slot_AV_T[bs]),
                                            ("V", pmsk_V)], xf)
                    if C_STEP < 4:
                        continue
                    if diag:
                        def mkf(v, bs=bs):
                            return v.tensor_mul(
                                Ptri[:, :, :],
                                AP(Pbuf, bs * 2048, [[4096, 128], [640, 4], [1, 128]]),
                                dmask_bf[:, :].rearrange("p (a b) -> p a b", a=4))
                        pmsk_V = emit("V", [("S", last_exp_S), ("T", ptri_T)], mkf)

                        def aj(t, b=b, h=h, bs=bs):
                            ins = None
                            ab4 = 4 + (h % 2)
                            for jj in range(4):
                                j = 4 * b + jj
                                q0 = jj * 128
                                ins = t.matmul(
                                    ps[0:65, ab4 * 512 + q0:ab4 * 512 + q0 + 128],
                                    Vst[:, h, j, 0:65], Ptri[:, jj, :],
                                    start=(b == 0 and jj == 0), stop=(jj == 3))
                                if jj < 3:
                                    ins = t.matmul(
                                        ps[0:65, ab4 * 512 + q0 + 128:(ab4 + 1) * 512],
                                        Vst[:, h, j, 0:65],
                                        Pbuf[:, bs, jj, q0 + 128:512],
                                        start=False, stop=False)
                            return ins
                        avw = [("V", pmsk_V)]
                        if b == 0:
                            avw.append(("V", attn_bank_evac_V[h % 2]))
                        b_slot_AV_T[bs] = emit("T", avw, aj)
                        ptri_T = b_slot_AV_T[bs]
                    else:
                        def af(t, b=b, bs=bs, h=h):
                            ins = None
                            ab4 = 4 + (h % 2)
                            for jj in range(4):
                                j = 4 * b + jj
                                ins = t.matmul(ps[0:65, ab4 * 512:(ab4 + 1) * 512],
                                               Vst[:, h, j, 0:65], Pbuf[:, bs, jj, :],
                                               start=(b == 0 and jj == 0),
                                               stop=False)
                            return ins
                        av_wait = [("S", last_exp_S)]
                        if b == 0:
                            av_wait.append(("V", attn_bank_evac_V[h % 2]))
                        b_slot_AV_T[bs] = emit("T", av_wait, af)
                if C_STEP < 5:
                    continue
                av_T = b_slot_AV_T[gs % 2]
                us = h % 4
                ab4 = 4 + (h % 2)

                def uf(v, us=us, ab4=ab4):
                    return v.tensor_copy(attnU[:, us, :],
                                         ps[0:65, ab4 * 512:(ab4 + 1) * 512])
                ue = emit("V", [("T", av_T), ("V", attnU_mult_G[us])], uf)
                attn_bank_evac_V[h % 2] = ue
                if C_STEP < 6:
                    continue

                def rsf(sy, us=us, h=h):
                    return sy.dma_start(rs_scr[h:h + 1, :], attnU[64:65, us, :])
                emit("SRS", [("V", ue), ("SRL", C["SRL"])], rsf)

                if C_STEP < 7:
                    continue
                if h % 4 == 3:
                    g4 = h // 4

                    def lf(sy, g4=g4):
                        return sy.dma_start(rs_sb[:, :], rs_scr[g4 * 4:(g4 + 1) * 4, :])
                    ld_sy = emit("SRL", [("S", rs_read_S), ("SRS", C["SRS"])], lf)
                    rs_read_S = emit("S", [("SRL", ld_sy)],
                                     lambda s: s.activation(rr_sb[:, :], rs_sb[:, :],
                                                            AF.Ln, bias=0.0, scale=1.0))
                    s2 = emit("S", [], lambda s: s.activation(
                        rr2_sb[:, :], rr_sb[:, :], AF.Exp, bias=0.0, scale=-1.0))

                    def stf(sy, g4=g4):
                        return sy.dma_start(rr_scr[g4 * 4:(g4 + 1) * 4, :], rr2_sb[:, :])
                    rr_stored_SY = emit("SRW", [("S", s2), ("SBC0", C["SBC0"]),
                                                ("SBC1", C["SBC1"])], stf)
                    if C_STEP < 8:
                        continue
                    for h2 in range(g4 * 4, g4 * 4 + 4):
                        bs2 = h2 % 2
                        hp2, ab2 = h2 // 2, h2 % 2
                        us2 = h2 % 4

                        def bf(sy, h2=h2, bs2=bs2):
                            return sy.dma_start(
                                bcast[:, bs2, :],
                                rr_scr[h2:h2 + 1, :].partition_broadcast(64))
                        bsy = emit("SBC%d" % bs2, [("V", bc_slot_G[bs2]),
                                                   ("SRW", C["SRW"])], bf)
                        if C_STEP < 9:
                            continue
                        if ab2 == 0:
                            dst = attnT[0:64, hp2, :]
                        else:
                            dst = attnB[:, :]

                        def mf(v, dst=dst, us2=us2, bs2=bs2):
                            return v.tensor_mul(dst, attnU[0:64, us2, :],
                                                bcast[:, bs2, :])
                        mw = [("SBC%d" % bs2, bsy), ("T", outproj_done_T)]
                        if ab2 == 1:
                            mw.append(("SSB", stageB_SY[0]))
                        mult_cnt[(ch, h2)] = emit("V", mw, mf)
                        bc_slot_G[bs2] = mult_cnt[(ch, h2)]
                        attnU_mult_G[us2] = mult_cnt[(ch, h2)]
                        if ab2 == 1:
                            def sbf(sy, hp2=hp2):
                                return sy.dma_start(attnT[64:128, hp2, :],
                                                    attnB[:, :])
                            stageB_SY[0] = emit(
                                "SSB", [("V", mult_cnt[(ch, h2)])], sbf)
            if (ch, 14) not in mult_cnt:
                continue
            attn_done_G = mult_cnt[(ch, 14)]
            attn_done_SY = max(stageB_SY)

            if "D" not in ph:
                continue
            d3_V = 0
            for ti in range(NTILE):
                def of(t, ti=ti):
                    ins = None
                    for hp in range(8):
                        for mg in range(2):
                            ins = t.matmul(ps[:, mg * 512:(mg + 1) * 512],
                                           attnT[:, hp, ti * 128:(ti + 1) * 128],
                                           woT[:, hp, mg * 512:(mg + 1) * 512],
                                           start=(hp == 0), stop=(hp == 7))
                    return ins
                ow = [("V", attn_done_G), ("SSB", attn_done_SY), ("S", last_exp_S),
                      ("V", d3_V)]
                outproj_done_T = emit("T", ow, of)
                op_T = outproj_done_T

                def df(v, ti=ti, slot=slot):
                    return v.tensor_add(x_sb[:, slot, ti, :], x_sb[:, slot, ti, :],
                                        ps[:, 0:1024])
                d3_V = emit("V", [("T", op_T)], df)

            if "E" not in ph:
                continue
            for ti in range(NTILE):
                ln_block(ti, [("V", d3_V)] if ti == 0 else [])
            e_done_V = last_tp_evac

            if "F" not in ph:
                continue
            def f1(t):
                ins = None
                for c in range(8):
                    ins = t.matmul(ps[0:64, 3072:3584], win_bf[:, c, :], xlnT[:, c, :],
                                   start=(c == 0), stop=(c == 7))
                return ins
            f1_T = emit("T", [("V", e_done_V), ("S", last_hS)], f1)
            for i in range(16):
                hs = i % 2

                def hf2(s, i=i, hs=hs):
                    return s.activation(h_sb[:, hs, :], ps[0:64, 3072:3584],
                                        AF.Relu, bias=b1p[:, i:i + 1],
                                        scale=w1_sb[:, i:i + 1])
                sH = emit("S", [("T", f1_T), ("V", hb_slot_V[hs])], hf2)
                last_hS = sH
                if i == 0:
                    mac_V = emit("V", [("S", sH)],
                                 lambda v: v.tensor_scalar(g_sb[:, 0, :], h_sb[:, 0, :],
                                                           w2_sb[:, 0:1], None, ALU.mult))
                else:
                    def mi(v, i=i, hs=hs):
                        return v.scalar_tensor_tensor(g_sb[:, i % 2, :], h_sb[:, hs, :],
                                                      w2_sb[:, i:i + 1],
                                                      g_sb[:, (i + 1) % 2, :],
                                                      ALU.mult, ALU.add)
                    mac_V = emit("V", [("S", sH)], mi)
                hb_slot_V[hs] = mac_V
            gb_V = emit("V", [], lambda v: v.tensor_copy(gbf[:, :], g_sb[:, 1, :]))
            f7_V = 0
            for ti in range(NTILE):
                def f6(t, ti=ti):
                    t.matmul(ps[:, 0:512], gbf[:, ti * 128:(ti + 1) * 128],
                             wout_bf[:, 0:512], start=True, stop=False)
                    t.matmul(ps[:, 512:1024], gbf[:, ti * 128:(ti + 1) * 128],
                             wout_bf[:, 512:1024], start=True, stop=False)
                    t.matmul(ps[:, 0:512], ones1[:, :], boute_bf[:, 0:512],
                             start=False, stop=True)
                    return t.matmul(ps[:, 512:1024], ones1[:, :], boute_bf[:, 512:1024],
                                    start=False, stop=True)
                f6_T = emit("T", [("V", gb_V), ("V", f7_V if ti > 0 else d3_V),
                                  ("G", g_pro)], f6)

                def f7(v, ti=ti, slot=slot):
                    return v.tensor_add(x_sb[:, slot, ti, :], x_sb[:, slot, ti, :],
                                        ps[:, 0:1024])
                f7_V = emit("V", [("T", f6_T)], f7)
            last_F7_V = f7_V

            off = ch * CH * D

            def f8(sy, off=off, slot=slot):
                return sy.dma_start(AP(outx, off, [[1024, 128], [131072, 4], [1, 1024]]),
                                    x_sb[:, slot, :, :])
            emit("SOUT", [("V", f7_V)], f8)
            if ch + 2 < NCH:
                xld[ch + 2] = emit("SX%d" % (ch % 2), [("SOUT", C["SOUT"])],
                                   x_load(ch + 2))

    if DBG:
        dw = [("V", C["V"]), ("S", C["S"]), ("G", C["G"]), ("T", C["T"])]
        emit("GP", dw, lambda g: g.dma_start(d_cbc[:, :], cbc[:, :]))
        emit("GP", [], lambda g: g.dma_start(d_qt[:, :], QT[:, :, :]))
        emit("GP", [], lambda g: g.dma_start(
            d_kt[:, :], AP(KT, 0, [[16384, 128], [2048, 8], [1, 512]])))
        emit("GP", [], lambda g: g.dma_start(
            d_vst[:, :], AP(Vst, 0, [[16640, 128], [1040, 16], [1, 260]])))
        emit("GP", [], lambda g: g.dma_start(
            d_au[:, :], AP(attnU, 0, [[2048, 65], [1, 2048]])))
        emit("GP", [], lambda g: g.dma_start(d_xlnT[:, :], xlnT[:, :, :]))
        emit("GP", [], lambda g: g.dma_start(
            d_p[:, :], AP(Pbuf, 0, [[4096, 128], [1, 4096]])))
        emit("GP", [], lambda g: g.dma_start(d_at[:, :], attnT[:, :, :]))
        emit("GP", [], lambda g: g.dma_start(
            d_x[:, :], AP(x_sb, 0, [[8192, 128], [1, 4096]])))
        emit("GP", [], lambda g: g.dma_start(d_wq[:, :], wq_sb[:, :, :, :]))
        emit("GP", [], lambda g: g.dma_start(d_wv[:, :], wv_sb[:, :, :, :]))
        emit("GP", [], lambda g: g.dma_start(d_xln[:, :], xln[:, :, :]))
        emit("GP", [], lambda g: g.dma_start(d_mv[:, :], mv[:, :, :]))
        emit("GP", [], lambda g: g.dma_start(d_rstd[:, :], rstd[:, :, :]))
        emit("GP", [], lambda g: g.dma_start(d_lnv[:, :], lnv[:, :, :]))
        rtest = al("rtest", [128, 1], F32)
        sr2 = emit("S", [("V", C["V"]), ("T", C["T"]), ("G", C["G"])],
                   lambda s: s.activation(rtest[:, :], lnv[:, 0, :], AF.Exp,
                                          bias=0.0, scale=-0.5))
        emit("GP", [("S", sr2)], lambda g: g.dma_start(d_r2[:, :], rtest[:, :]))
        emit("GP", [], lambda g: g.dma_start(d_eps[:, 0:1], eps_t[:, :], allow_slow_non_contiguous=True))
        emit("GP", [], lambda g: g.dma_start(d_eps[:, 1:2], ln8_t[:, :], allow_slow_non_contiguous=True))

    final_sy = C["SOUT"]

    def fin(eng):
        eng.wait_ge(SEM["SOUT"], final_sy)
        eng.nop()
    plan["sync"].append(fin)

    with nc.Block() as block:
        @block.tensor
        def _(t):
            for f in plan["tensor"]:
                f(t)

        @block.vector
        def _(v):
            for f in plan["vector"]:
                f(v)

        @block.scalar
        def _(s):
            for f in plan["scalar"]:
                f(s)

        @block.gpsimd
        def _(g):
            for f in plan["gpsimd"]:
                f(g)

        @block.sync
        def _(sy):
            for f in plan["sync"]:
                f(sy)

    return nc


def _host_layouts(inputs):
    w_qkv = np.asarray(inputs["L0_attn_w_qkv"], dtype=np.float32)
    w_o = np.asarray(inputs["L0_attn_w_o"], dtype=np.float32)
    Win = np.asarray(inputs["L0_ffn_Win"], dtype=np.float32)
    bin_ = np.asarray(inputs["L0_ffn_bin_"], dtype=np.float32)
    W1 = np.asarray(inputs["L0_ffn_W1"], dtype=np.float32)
    b1 = np.asarray(inputs["L0_ffn_b1"], dtype=np.float32)
    W2 = np.asarray(inputs["L0_ffn_W2"], dtype=np.float32)
    b2 = np.asarray(inputs["L0_ffn_b2"], dtype=np.float32)
    Wout = np.asarray(inputs["L0_ffn_Wout"], dtype=np.float32)
    bout = np.asarray(inputs["L0_ffn_bout"], dtype=np.float32)

    wq = w_qkv[0].reshape(8, 128, 1024)
    wk = w_qkv[1].reshape(8, 128, 1024)
    wqk = np.concatenate([wq, wk], axis=0)
    wqk = wqk.transpose(0, 2, 1).reshape(16, 8, 128, 128)
    wvm = w_qkv[2].reshape(1024, 1024).T
    wv = wvm.reshape(8, 128, 2, 512).transpose(2, 0, 1, 3)
    wo = w_o.T.reshape(8, 128, 1024)
    win = Win.reshape(8, 128, 64)

    ident = np.eye(128, dtype=np.float32)
    p = np.arange(128)[:, None]
    r = np.arange(128)[None, :]
    tri = (p <= r).astype(np.float32)
    dmask = np.tile(tri, (1, 4))

    return {
        "wqk": np.ascontiguousarray(wqk),
        "wv": np.ascontiguousarray(wv),
        "wo": np.ascontiguousarray(wo),
        "win": np.ascontiguousarray(win),
        "w1": np.ascontiguousarray(W1),
        "b1": np.ascontiguousarray(b1),
        "w2": np.ascontiguousarray(W2),
        "binv": np.ascontiguousarray(bin_.reshape(64, 1)),
        "b2v": np.ascontiguousarray(b2.reshape(64, 1)),
        "boutv": np.ascontiguousarray(bout.reshape(1, 1024)),
        "woutm": np.ascontiguousarray(Wout),
        "ident": ident,
        "dmask": np.ascontiguousarray(dmask),
    }


def kernel(**inputs):
    x = np.asarray(inputs["L0_attn_x"], dtype=np.float32)
    shared = _host_layouts(inputs)
    nc = _plan_program()
    in_maps = []
    for c in range(NCORES):
        m = dict(shared)
        m["xin"] = np.ascontiguousarray(x[c * BPC:(c + 1) * BPC].reshape(BPC * S, D))
        in_maps.append(m)
    res = run_bass_kernel_spmd(nc, in_maps, core_ids=list(range(NCORES)))
    out = np.concatenate(
        [np.asarray(r["outx"]).reshape(BPC, S, D) for r in res.results], axis=0)
    return out
